# version 01
# baseline (speedup 1.0000x reference)
import os
import numpy as np
import concourse.bass as bass
import concourse.mybir as mybir
from concourse.bass_utils import run_bass_kernel_spmd

F32 = mybir.dt.float32
BF16 = mybir.dt.bfloat16
AF = mybir.ActivationFunctionType
ALU = mybir.AluOpType
AX = mybir.AxisListType.X

NCORE = 8
B = 4
L = 8192
LC = L // NCORE
DM = 1024
DI = 2048
NS = 16
DTR = 64
EXP = 8
HF = 2048
DSL = DI // NCORE
L2T = 2048
EPS = 1e-5


class Cnt:
    def __init__(self, h):
        self.h = h
        self.n = 0


class Prog:

    ENGS = ("sy", "gp", "te", "sc", "v")

    def __init__(self):
        self.q = {k: [] for k in self.ENGS}
        self.waited = {}

    def emit(self, ek, builder, waits=(), incs=()):
        rw = []
        for c, n in waits:
            n = c.n if n is None else n
            if n <= 0:
                continue
            key = (ek, id(c))
            if self.waited.get(key, 0) >= n:
                continue
            self.waited[key] = n
            rw.append((c.h, n))
        ri = []
        for c, k in incs:
            c.n += k
            ri.append((c.h, k))

        def fn(eng, rw=tuple(rw), ri=tuple(ri), builder=builder):
            for h, n in rw:
                eng.wait_ge(h, n)
            ins = builder(eng)
            for h, k in ri:
                ins.then_inc(h, k)
            return ins

        self.q[ek].append(fn)

    def run(self, block):
        def mk(lst):
            def go(eng):
                for f in lst:
                    f(eng)
            return go
        block.sync(mk(self.q["sy"]))
        block.gpsimd(mk(self.q["gp"]))
        block.tensor(mk(self.q["te"]))
        block.scalar(mk(self.q["sc"]))
        block.vector(mk(self.q["v"]))


def WT(c, n=None):
    return (c, n)


def IC(c, k=1):
    return (c, k)


def _sems(nc, ctx, names):
    return {nm: Cnt(ctx.enter_context(nc.semaphore(nm))) for nm in names}


def build_l1():
    L1T = int(os.environ.get("L1T", "99"))
    nc = bass.Bass(target_bir_lowering=False)
    x = nc.dram_tensor("x", [B, LC, DM], F32, kind="ExternalInput")
    xh = nc.dram_tensor("xh", [B, 3, DM], F32, kind="ExternalInput")
    w_in = nc.dram_tensor("w_in", [DM, 2 * DI], F32, kind="ExternalInput")
    convw = nc.dram_tensor("convw", [128, 64], F32, kind="ExternalInput")
    convb = nc.dram_tensor("convb", [128, 16], F32, kind="ExternalInput")
    w_x = nc.dram_tensor("w_x", [128, 16 * 96], F32, kind="ExternalInput")
    w_dt = nc.dram_tensor("w_dt", [DTR, DI], F32, kind="ExternalInput")
    b_dt = nc.dram_tensor("b_dt", [128, 16], F32, kind="ExternalInput")
    a2a = nc.dram_tensor("a2a", [B, 2 * DI, LC], F32, kind="ExternalOutput")
    bc = nc.dram_tensor("bc", [B, 2 * NS, LC], F32, kind="ExternalOutput")
    sz = nc.dram_tensor("sz", [B, DI, LC], F32, kind="ExternalOutput")

    from contextlib import ExitStack
    with ExitStack() as ctx:
        def sb(name, shp, dt=F32):
            return ctx.enter_context(nc.sbuf_tensor(name, shp, dt))

        def ps(name, shp, dt=F32):
            return ctx.enter_context(nc.psum_tensor(name, shp, dt))

        winb = sb("winb", [128, 8, 2 * DI], BF16)
        wxb = sb("wxb", [128, 16, 96], BF16)
        wdtb = sb("wdtb", [DTR, DI], BF16)
        convw_s = sb("convw_s", [128, 16, 4], F32)
        convb_s = sb("convb_s", [128, 16], F32)
        bdt_s = sb("bdt_s", [128, 16], F32)
        ident = sb("ident", [128, 128], BF16)
        x_tl = [sb(f"x_tl{i}", [128, DM], F32) for i in range(2)]
        xn_bf = sb("xn_bf", [128, 8, DM], BF16)
        xnT = sb("xnT", [128, 8, LC], BF16)
        u_b2 = [sb(f"u_b2{i}", [128, LC], BF16) for i in range(2)]
        u_ext = sb("u_ext", [128, 3 + LC], F32)
        u_f32 = sb("u_f32", [128, LC], F32)
        zsil = sb("zsil", [128, LC], F32)
        cacc = sb("cacc", [128, LC], F32)
        xd = sb("xd", [96, LC], F32)
        dtlr_bf = sb("dtlr_bf", [DTR, LC], BF16)
        t_a = sb("t_a", [128, LC], F32)
        t_b = sb("t_b", [128, LC], F32)
        t_c = sb("t_c", [128, LC], F32)
        dt_f = sb("dt_f", [128, LC], F32)
        xh_t = sb("xh_t", [12, DM], F32)
        xh_bf = sb("xh_bf", [12, DM], BF16)
        xhT = sb("xhT", [128, 96], BF16)
        u_halo = sb("u_halo", [128, 16, 12], F32)
        s1 = sb("s1", [128, 8], F32)
        s2 = sb("s2", [128, 8], F32)
        sqs = sb("sqs", [128, DM], F32)
        mcol = sb("mcol", [128, 8], F32)
        vcol = sb("vcol", [128, 8], F32)
        rcol = sb("rcol", [128, 8], F32)
        tcol = sb("tcol", [128, 8], F32)
        pT = ps("pT", [128, 1024], BF16)
        pm = [ps("pm0", [128, 512], F32), ps("pm1", [128, 512], F32)]
        pxd = ps("pxd", [128, 1024], F32)

        sm = _sems(nc, ctx, ["LD", "LG", "S", "V", "T", "G", "SO"])
        LD, LG, S, V, T, G, SO = (sm[k] for k in
                                  ["LD", "LG", "S", "V", "T", "G", "SO"])
        block = ctx.enter_context(nc.Block())
        p = Prog()
        pmg = [[], []]
        ub2g = [[], []]
        pxdg = []

        p.emit("gp", lambda g: g.dma_start(
            out=winb[:], in_=w_in.ap().rearrange("(kc p) e -> p kc e", p=128)),
            incs=[IC(LG, 16)])
        p.emit("gp", lambda g: g.dma_start(
            out=wxb[:], in_=w_x.ap().rearrange("p (kt r) -> p kt r", r=96)),
            incs=[IC(LG, 16)])
        p.emit("gp", lambda g: g.dma_start(out=wdtb[:], in_=w_dt.ap()),
               incs=[IC(LG, 16)])
        p.emit("gp", lambda g: g.memset(ident[:], 0.0), incs=[IC(G)])
        p.emit("gp", lambda g: g.affine_select(
            out=ident[:], in_=ident[:], compare_op=ALU.not_equal, fill=1.0,
            base=0, pattern=[[-1, 128]], channel_multiplier=1),
            waits=[WT(G)], incs=[IC(G)])
        p.emit("sy", lambda s: s.dma_start(
            out=convw_s[:], in_=convw.ap().rearrange("p (et t) -> p et t", t=4)),
            incs=[IC(LD, 16)])
        p.emit("sy", lambda s: s.dma_start(out=convb_s[:], in_=convb.ap()),
               incs=[IC(LD, 16)])
        p.emit("sy", lambda s: s.dma_start(out=bdt_s[:], in_=b_dt.ap()),
               incs=[IC(LD, 16)])
        p.emit("sy", lambda s: s.dma_start(
            out=xh_t[:], in_=xh.ap().rearrange("b t d -> (b t) d")),
            incs=[IC(LD, 16)])

        def emit_ln(src_ap, npar, dcount, out_ap, j, extra_waits=()):
            p.emit("v", lambda v, a=src_ap, jj=j, np_=npar: v.reduce_sum(
                s1[0:np_, jj:jj + 1], a, axis=AX),
                waits=list(extra_waits), incs=[IC(V)])
            p.emit("sc", lambda s, a=src_ap, jj=j, np_=npar: s.activation(
                sqs[0:np_, :], a, AF.Square, accum_out=s2[0:np_, jj:jj + 1]),
                waits=list(extra_waits), incs=[IC(S)])
            p.emit("v", lambda v, jj=j, np_=npar, d=dcount: v.tensor_scalar_mul(
                mcol[0:np_, jj:jj + 1], s1[0:np_, jj:jj + 1], 1.0 / d),
                waits=[WT(V)], incs=[IC(V)])
            p.emit("v", lambda v, jj=j, np_=npar, d=dcount: v.tensor_scalar_mul(
                vcol[0:np_, jj:jj + 1], s2[0:np_, jj:jj + 1], 1.0 / d),
                waits=[WT(S)], incs=[IC(V)])
            p.emit("v", lambda v, jj=j, np_=npar: v.tensor_scalar(
                tcol[0:np_, jj:jj + 1], mcol[0:np_, jj:jj + 1],
                mcol[0:np_, jj:jj + 1], None, ALU.mult),
                waits=[WT(V)], incs=[IC(V)])
            p.emit("v", lambda v, jj=j, np_=npar: v.tensor_sub(
                vcol[0:np_, jj:jj + 1], vcol[0:np_, jj:jj + 1],
                tcol[0:np_, jj:jj + 1]),
                waits=[WT(V)], incs=[IC(V)])
            p.emit("v", lambda v, jj=j, np_=npar: v.tensor_scalar_add(
                vcol[0:np_, jj:jj + 1], vcol[0:np_, jj:jj + 1], EPS),
                waits=[WT(V)], incs=[IC(V)])
            p.emit("sc", lambda s, jj=j, np_=npar: s.activation(
                rcol[0:np_, jj:jj + 1], vcol[0:np_, jj:jj + 1], AF.Ln),
                waits=[WT(V)], incs=[IC(S)])
            p.emit("sc", lambda s, jj=j, np_=npar: s.activation(
                rcol[0:np_, jj:jj + 1], rcol[0:np_, jj:jj + 1], AF.Exp,
                scale=-0.5),
                waits=[WT(S)], incs=[IC(S)])
            p.emit("v", lambda v, a=src_ap, o=out_ap, jj=j, np_=npar:
                   v.tensor_scalar(o, a, mcol[0:np_, jj:jj + 1],
                                   rcol[0:np_, jj:jj + 1],
                                   ALU.subtract, ALU.mult),
                   waits=[WT(S)], incs=[IC(V)])

        if L1T < 1:
            p.emit("sy", lambda s: None, waits=[WT(SO)])
            p.run(block)
            return nc
        emit_ln(xh_t[0:12, :], 12, DM, xh_bf[0:12, :], 0,
                extra_waits=[WT(LD)])
        for kc in range(8):
            p.emit("te", lambda t, kc=kc: t.transpose(
                pT[:, kc * 12:(kc + 1) * 12], xh_bf[0:12, bass.ts(kc, 128)],
                ident[0:12, 0:12]),
                waits=[WT(V), WT(G)], incs=[IC(T)] if kc == 7 else ())
        p.emit("sc", lambda s: s.copy(xhT[:, 0:96], pT[:, 0:96]),
               waits=[WT(T)], incs=[IC(S)])
        for et in range(16):
            pi = et % 2
            def halo_mm(t, et=et, pi=pi):
                ins = None
                for kc in range(8):
                    ins = t.matmul(pm[pi][:, 0:12],
                                   winb[:, kc, bass.ts(et, 128)],
                                   xhT[:, kc * 12:(kc + 1) * 12],
                                   start=(kc == 0), stop=(kc == 7))
                return ins
            p.emit("te", halo_mm,
                   waits=[WT(S), WT(LG)] + pmg[pi], incs=[IC(T)])
            p.emit("sc", lambda s, et=et, pi=pi: s.copy(
                u_halo[:, et, :], pm[pi][:, 0:12]),
                waits=[WT(T)], incs=[IC(S)])
            pmg[pi] = [WT(S)]

        if L1T < 2:
            p.emit("sy", lambda s: None, waits=[WT(SO)])
            p.run(block)
            return nc
        NW = int(os.environ.get("L1W", str(B)))
        for b in range(NW):
            for t in range(8):
                p.emit("sy", lambda s, b=b, t=t: s.dma_start(
                    out=x_tl[t % 2][:],
                    in_=x.ap()[b].rearrange("(t p) d -> p t d", p=128)[:, t, :]),
                    waits=[WT(V), WT(S)], incs=[IC(LD, 16)])
                emit_ln(x_tl[t % 2][:], 128, DM, xn_bf[:, t, :], t,
                        extra_waits=[WT(LD)])
            xn_done = Cnt.__new__(Cnt)
            xn_v = V.n
            if L1T < 3:
                continue
            for t in range(8):
                for kc in range(8):
                    p.emit("te", lambda te, t=t, kc=kc: te.transpose(
                        pT[:, bass.ts(kc, 128)],
                        xn_bf[:, t, bass.ts(kc, 128)], ident[:]),
                        waits=[WT(V, xn_v), WT(S)],
                        incs=[IC(T)] if kc == 7 else ())
                for kc in range(8):
                    p.emit("sc", lambda s, t=t, kc=kc: s.copy(
                        xnT[:, kc, bass.ts(t, 128)], pT[:, bass.ts(kc, 128)]),
                        waits=[WT(T)], incs=[IC(S)])
            xnT_s = S.n
            if L1T < 4:
                continue

            et_range = range(16, 32) if L1T == 4 else range(32)
            for et in et_range:
                for f in range(2):
                    pi = (et * 2 + f) % 2
                    def xzmm(te, et=et, f=f, pi=pi):
                        ins = None
                        for kc in range(8):
                            ins = te.matmul(
                                pm[pi][:],
                                winb[:, kc, bass.ts(et, 128)],
                                xnT[:, kc, bass.ts(f, 512)],
                                start=(kc == 0), stop=(kc == 7))
                        return ins
                    p.emit("te", xzmm,
                           waits=[WT(S, xnT_s), WT(LG, 48)] + pmg[pi],
                           incs=[IC(T)])
                    if et >= 16:
                        p.emit("sc", lambda s, f=f, pi=pi: s.activation(
                            zsil[:, bass.ts(f, 512)], pm[pi][:], AF.Silu),
                            waits=[WT(T), WT(SO)], incs=[IC(S)])
                        pmg[pi] = [WT(S)]
                    else:
                        p.emit("sc", lambda s, f=f, pi=pi: s.copy(
                            u_ext[:, 3 + f * 512: 3 + (f + 1) * 512], pm[pi][:]),
                            waits=[WT(T), WT(V)], incs=[IC(S)])
                        pmg[pi] = [WT(S)]
                if et >= 16:
                    p.emit("sy", lambda s, b=b, et=et: s.dma_start(
                        out=sz.ap()[b, (et - 16) * 128:(et - 15) * 128, :],
                        in_=zsil[:]),
                        waits=[WT(S)], incs=[IC(SO, 16)])
                else:
                    p.emit("v", lambda v, b=b, et=et: v.tensor_copy(
                        u_ext[:, 0:3], u_halo[:, et, b * 3:(b + 1) * 3]),
                        waits=[WT(S)], incs=[IC(V)])
                    p.emit("v", lambda v, et=et: v.tensor_scalar(
                        cacc[:], u_ext[:, 0:LC], convw_s[:, et, 0:1], None,
                        ALU.mult),
                        waits=[WT(V), WT(LD), WT(S)], incs=[IC(V)])
                    for tap in range(1, 4):
                        p.emit("v", lambda v, et=et, tap=tap:
                               v.scalar_tensor_tensor(
                                   cacc[:], u_ext[:, tap:tap + LC],
                                   convw_s[:, et, tap:tap + 1], cacc[:],
                                   ALU.mult, ALU.add),
                               waits=[WT(V)], incs=[IC(V)])
                    p.emit("sc", lambda s, et=et: s.activation(
                        u_f32[:], cacc[:], AF.Silu,
                        bias=convb_s[:, et:et + 1]),
                        waits=[WT(V), WT(SO)], incs=[IC(S)])
                    p.emit("v", lambda v, et=et: v.tensor_copy(
                        u_b2[et % 2][:], u_f32[:]),
                        waits=[WT(S)] + ub2g[et % 2], incs=[IC(V)])
                    ub2_v = V.n
                    def xdmm(te, et=et, ub2_v=ub2_v):
                        ins = None
                        for f in range(2):
                            ins = te.matmul(pxd[0:96, bass.ts(f, 512)],
                                            wxb[:, et, :],
                                            u_b2[et % 2][:, bass.ts(f, 512)],
                                            start=(et == 0), stop=(et == 15))
                        return ins
                    p.emit("te", xdmm,
                           waits=([WT(V, ub2_v), WT(LG, 48)]
                                  + (pxdg if et == 0 else [])),
                           incs=[IC(T)])
                    ub2g[et % 2] = [WT(T)]
                    row = (et // 2) * 512 + (et % 2) * 128
                    p.emit("sy", lambda s, b=b, row=row: s.dma_start(
                        out=a2a.ap()[b, row:row + 128, :], in_=u_f32[:]),
                        waits=[WT(V)], incs=[IC(SO, 16)])

            if L1T < 6:
                continue
            for f in range(2):
                p.emit("sc", lambda s, f=f: s.copy(
                    xd[:, bass.ts(f, 512)], pxd[0:96, bass.ts(f, 512)]),
                    waits=[WT(T), WT(SO)], incs=[IC(S)])
            pxdg = [WT(S)]
            p.emit("v", lambda v: v.tensor_copy(dtlr_bf[:], xd[0:64, :]),
                   waits=[WT(S)], incs=[IC(V)])
            dtlr_v = V.n
            p.emit("sy", lambda s, b=b: s.dma_start(
                out=bc.ap()[b], in_=xd[64:96, :]),
                waits=[WT(S)], incs=[IC(SO, 16)])

            if L1T < 7:
                continue
            L1DT = int(os.environ.get("L1DT", "2"))
            for mt in range(16):
                for f in range(2):
                    pi = (mt * 2 + f) % 2
                    p.emit("te", lambda te, mt=mt, f=f, pi=pi: te.matmul(
                        pm[pi][:], wdtb[0:64, bass.ts(mt, 128)],
                        dtlr_bf[0:64, bass.ts(f, 512)],
                        start=True, stop=True),
                        waits=[WT(V, dtlr_v), WT(LG, 48)] + pmg[pi],
                        incs=[IC(T)])
                    fs = bass.ts(f, 512)
                    if L1DT == 0:
                        p.emit("v", lambda v, pi=pi, fs=fs: v.tensor_copy(
                            dt_f[:, fs], pm[pi][:]),
                            waits=[WT(T), WT(SO)], incs=[IC(V)])
                        pmg[pi] = [WT(V)]
                        continue
                    p.emit("v", lambda v, mt=mt, pi=pi, fs=fs: v.tensor_scalar(
                        t_a[:, fs], pm[pi][:], bdt_s[:, mt:mt + 1], None,
                        ALU.add),
                        waits=[WT(T), WT(SO)], incs=[IC(V)])
                    pmg[pi] = [WT(V)]
                    p.emit("sc", lambda s, fs=fs: s.activation(
                        t_b[:, fs], t_a[:, fs], AF.Abs),
                        waits=[WT(V)], incs=[IC(S)])
                    p.emit("sc", lambda s, fs=fs: s.activation(
                        t_c[:, fs], t_b[:, fs], AF.Exp, scale=-1.0),
                        waits=[WT(S)], incs=[IC(S)])
                    p.emit("sc", lambda s, fs=fs: s.activation(
                        t_c[:, fs], t_c[:, fs], AF.Ln, bias=1.0),
                        waits=[WT(S)], incs=[IC(S)])
                    p.emit("v", lambda v, fs=fs: v.scalar_tensor_tensor(
                        dt_f[:, fs], t_a[:, fs], 0.0, t_c[:, fs],
                        ALU.max, ALU.add),
                        waits=[WT(S)], incs=[IC(V)])
                if L1DT >= 2:
                    row = (mt // 2) * 512 + 256 + (mt % 2) * 128
                    p.emit("sy", lambda s, b=b, row=row: s.dma_start(
                        out=a2a.ap()[b, row:row + 128, :], in_=dt_f[:]),
                        waits=[WT(V)], incs=[IC(SO, 16)])

        p.emit("sy", lambda s: None, waits=[WT(SO)])
        p.run(block)
    return nc


def build_l2():
    L2P = int(os.environ.get("L2P", "99"))
    nc = bass.Bass(target_bir_lowering=False)
    u_s = nc.dram_tensor("u_s", [B, DSL, L], F32, kind="ExternalInput")
    dt_s = nc.dram_tensor("dt_s", [B, DSL, L], F32, kind="ExternalInput")
    b_all = nc.dram_tensor("b_all", [B, NS, L], F32, kind="ExternalInput")
    c_all = nc.dram_tensor("c_all", [B, NS, L], F32, kind="ExternalInput")
    alog = nc.dram_tensor("alog", [1, NS], F32, kind="ExternalInput")
    d_c = nc.dram_tensor("d_c", [128, 2], F32, kind="ExternalInput")
    y_s = nc.dram_tensor("y_s", [B, DSL, L], F32, kind="ExternalOutput")
    a_scr = nc.dram_tensor("a_scr", [1, NS], F32)

    from contextlib import ExitStack
    with ExitStack() as ctx:
        def sb(name, shp, dt=F32):
            return ctx.enter_context(nc.sbuf_tensor(name, shp, dt))

        scr = nc.dram_tensor("scr", [1, 4 * NS, L2T], BF16)
        u_t = [sb(f"u{g}", [128, L2T]) for g in range(2)]
        dt_t = [sb(f"dt{g}", [128, L2T]) for g in range(2)]
        dtu = [sb(f"dtu{g}", [128, L2T]) for g in range(2)]
        abar = [sb(f"abar{g}", [128, L2T]) for g in range(2)]
        beta = sb("beta", [128, L2T])
        hbuf = [sb(f"h{g}", [128, L2T]) for g in range(2)]
        hcb = sb("hcb", [128, L2T])
        yac = [sb(f"y{g}", [128, L2T]) for g in range(2)]
        carry = [sb(f"carry{g}", [128, NS]) for g in range(2)]
        bt = sb("bt", [NS, L2T])
        ct = sb("ct", [NS, L2T])
        bhi = sb("bhi", [NS, L2T], BF16)
        blo = sb("blo", [NS, L2T], BF16)
        chi = sb("chi", [NS, L2T], BF16)
        clo = sb("clo", [NS, L2T], BF16)
        bhif = sb("bhif", [NS, L2T])
        chif = sb("chif", [NS, L2T])
        blof = sb("blof", [NS, L2T])
        clof = sb("clof", [NS, L2T])
        selmat = sb("selmat", [NS, NS * 128], BF16)
        alog_s = sb("alog_s", [1, NS])
        aneg = sb("aneg", [1, NS])
        acols = sb("acols", [128, NS])
        d_sb = sb("d_sb", [128, 2])
        pb = ctx.enter_context(nc.psum_tensor("pb", [128, L2T], F32))
        pc = ctx.enter_context(nc.psum_tensor("pc", [128, L2T], F32))

        sm = _sems(nc, ctx, ["LD", "LG", "S", "V", "T", "G", "SO"])
        LD, LG, S, V, T, G, SO = (sm[k] for k in
                                  ["LD", "LG", "S", "V", "T", "G", "SO"])
        block = ctx.enter_context(nc.Block())
        p = Prog()

        p.emit("sy", lambda s: s.dma_start(out=alog_s[:], in_=alog.ap()),
               incs=[IC(LD, 16)])
        p.emit("sy", lambda s: s.dma_start(out=d_sb[:], in_=d_c.ap()),
               incs=[IC(LD, 16)])
        p.emit("sc", lambda s: s.activation(aneg[:], alog_s[:], AF.Exp),
               waits=[WT(LD)], incs=[IC(S)])
        p.emit("v", lambda v: v.tensor_scalar_mul(aneg[:], aneg[:], -1.0),
               waits=[WT(S)], incs=[IC(V)])
        p.emit("gp", lambda g: g.dma_start(out=a_scr.ap(), in_=aneg[:]),
               waits=[WT(V)], incs=[IC(LG, 16)])
        p.emit("gp", lambda g: g.dma_start(
            out=acols[:], in_=a_scr.ap().broadcast_to([128, NS])),
            waits=[WT(LG)], incs=[IC(LG, 16)])
        acols_lg = LG.n
        p.emit("gp", lambda g: g.memset(selmat[:], 1.0), incs=[IC(G)])
        p.emit("gp", lambda g: g.affine_select(
            out=selmat[:], in_=selmat[:], compare_op=ALU.is_ge, fill=0.0,
            base=0, channel_multiplier=-128, pattern=[[1, NS * 128]]),
            waits=[WT(G)], incs=[IC(G)])
        p.emit("gp", lambda g: g.affine_select(
            out=selmat[:], in_=selmat[:], compare_op=ALU.is_ge, fill=0.0,
            base=127, channel_multiplier=128, pattern=[[-1, NS * 128]]),
            waits=[WT(G)], incs=[IC(G)])

        pbg = []
        pcg = []
        for b in range(B if L2P >= 1 else 0):
            for lt in range(4):
                ls = bass.ts(lt, L2T)
                ld_mark = []
                for g in range(2):
                    p.emit("sy", lambda s, b=b, g=g, ls=ls: s.dma_start(
                        out=u_t[g][:], in_=u_s.ap()[b, g * 128:(g + 1) * 128, ls]),
                        waits=[WT(V), WT(SO)], incs=[IC(LD, 16)])
                    p.emit("sy", lambda s, b=b, g=g, ls=ls: s.dma_start(
                        out=dt_t[g][:], in_=dt_s.ap()[b, g * 128:(g + 1) * 128, ls]),
                        waits=[WT(V), WT(S)], incs=[IC(LD, 16)])
                p.emit("sy", lambda s, b=b, ls=ls: s.dma_start(
                    out=bt[:], in_=b_all.ap()[b][:, ls]),
                    waits=[WT(V), WT(LG)], incs=[IC(LD, 16)])
                p.emit("sy", lambda s, b=b, ls=ls: s.dma_start(
                    out=ct[:], in_=c_all.ap()[b][:, ls]),
                    waits=[WT(V)], incs=[IC(LD, 16)])
                tiles_ld = LD.n

                p.emit("gp", lambda g: g.dma_start(
                    out=scr.ap()[0, 0:NS, :], in_=bt[:]),
                    waits=[WT(LD, tiles_ld), WT(T)], incs=[IC(LG, 16)])
                p.emit("gp", lambda g: g.dma_start(
                    out=scr.ap()[0, 2 * NS:3 * NS, :], in_=ct[:]),
                    incs=[IC(LG, 16)])
                hi_lg = LG.n
                p.emit("gp", lambda g: g.dma_start(
                    out=bhif[:], in_=scr.ap()[0, 0:NS, :]),
                    waits=[WT(LG, hi_lg)], incs=[IC(LG, 16)])
                p.emit("gp", lambda g: g.dma_start(
                    out=chif[:], in_=scr.ap()[0, 2 * NS:3 * NS, :]),
                    incs=[IC(LG, 16)])
                p.emit("gp", lambda g: g.dma_start(
                    out=bhi[:], in_=scr.ap()[0, 0:NS, :]),
                    incs=[IC(LG, 16)])
                p.emit("gp", lambda g: g.dma_start(
                    out=chi[:], in_=scr.ap()[0, 2 * NS:3 * NS, :]),
                    incs=[IC(LG, 16)])
                hif_lg = LG.n
                p.emit("v", lambda v: v.tensor_sub(blof[:], bt[:], bhif[:]),
                       waits=[WT(LG, hif_lg)], incs=[IC(V)])
                p.emit("v", lambda v: v.tensor_sub(clof[:], ct[:], chif[:]),
                       waits=[WT(LG, hif_lg)], incs=[IC(V)])
                lof_v = V.n
                p.emit("v", lambda v: v.tensor_copy(blo[:], blof[:]),
                       waits=[WT(V, lof_v), WT(T)], incs=[IC(V)])
                p.emit("v", lambda v: v.tensor_copy(clo[:], clof[:]),
                       incs=[IC(V)])
                pack_v = V.n

                if L2P < 2:
                    continue
                for g in range(2):
                    p.emit("v", lambda v, g=g: v.tensor_mul(
                        dtu[g][:], dt_t[g][:], u_t[g][:]),
                        waits=[WT(LD, tiles_ld)], incs=[IC(V)])

                for n in range(NS if L2P >= 3 else 0):
                    def bcast(te, n, dst, hi, lo):
                        ins = None
                        for fs in range(4):
                            for part, src in ((0, hi), (1, lo)):
                                ins = te.matmul(
                                    dst[:, fs * 512:(fs + 1) * 512],
                                    selmat[:, bass.ts(n, 128)],
                                    src[:, fs * 512:(fs + 1) * 512],
                                    start=(part == 0), stop=(part == 1))
                        return ins
                    p.emit("te", lambda te, n=n: bcast(te, n, pb, bhi, blo),
                           waits=[WT(V, pack_v), WT(G)] + pbg, incs=[IC(T)])
                    p.emit("te", lambda te, n=n: bcast(te, n, pc, chi, clo),
                           waits=pcg, incs=[IC(T)])
                    if L2P < 4:
                        pbg = [WT(V)]
                        pcg = [WT(V)]
                        continue
                    for g in range(2):
                        p.emit("sc", lambda s, n=n, g=g: s.activation(
                            abar[g][:], dt_t[g][:], AF.Exp,
                            scale=acols[:, n:n + 1]),
                            waits=[WT(LD, tiles_ld), WT(LG, acols_lg), WT(V)],
                            incs=[IC(S)])
                    for g in range(2 if L2P >= 5 else 0):
                        p.emit("v", lambda v, g=g: v.tensor_tensor(
                            beta[:], dtu[g][:], pb[:], ALU.mult),
                            waits=[WT(T), WT(V)], incs=[IC(V)])
                        init = 0.0 if lt == 0 else carry[g][:, n:n + 1]
                        p.emit("v", lambda v, g=g, n=n, init=init:
                               v.tensor_tensor_scan(
                                   hbuf[g][:], abar[g][:], beta[:], init,
                                   ALU.mult, ALU.add),
                               waits=[WT(S), WT(V)], incs=[IC(V)])
                        p.emit("v", lambda v, g=g, n=n: v.tensor_copy(
                            carry[g][:, n:n + 1], hbuf[g][:, L2T - 1:L2T]),
                            waits=[WT(V)], incs=[IC(V)])
                        if L2P < 6:
                            continue
                        if n == 0:
                            p.emit("v", lambda v, g=g: v.tensor_tensor(
                                yac[g][:], hbuf[g][:], pc[:], ALU.mult),
                                waits=[WT(T), WT(V), WT(SO)], incs=[IC(V)])
                        else:
                            p.emit("v", lambda v, g=g: v.tensor_tensor(
                                hcb[:], hbuf[g][:], pc[:], ALU.mult),
                                waits=[WT(T), WT(V)], incs=[IC(V)])
                            p.emit("v", lambda v, g=g: v.tensor_add(
                                yac[g][:], yac[g][:], hcb[:]),
                                waits=[WT(V)], incs=[IC(V)])
                    pbg = [WT(V)]
                    pcg = [WT(V)]
                if L2P < 7:
                    continue
                for g in range(2):
                    p.emit("v", lambda v, g=g: v.scalar_tensor_tensor(
                        yac[g][:], u_t[g][:], d_sb[:, g:g + 1], yac[g][:],
                        ALU.mult, ALU.add),
                        waits=[WT(V), WT(LD)], incs=[IC(V)])
                    p.emit("sy", lambda s, b=b, g=g, ls=ls: s.dma_start(
                        out=y_s.ap()[b, g * 128:(g + 1) * 128, ls],
                        in_=yac[g][:]),
                        waits=[WT(V)], incs=[IC(SO, 16)])

        p.emit("sy", lambda s: None, waits=[WT(SO)])
        p.run(block)
    return nc


def build_l3():
    nc = bass.Bass(target_bir_lowering=False)
    x = nc.dram_tensor("x", [B, LC, DM], F32, kind="ExternalInput")
    y_in = nc.dram_tensor("y_in", [B, DI, LC], F32, kind="ExternalInput")
    sz = nc.dram_tensor("sz", [B, DI, LC], F32, kind="ExternalInput")
    w_out = nc.dram_tensor("w_out", [DI, DM], F32, kind="ExternalInput")
    router_t = nc.dram_tensor("router_t", [DM, EXP], F32, kind="ExternalInput")
    up_t = nc.dram_tensor("up_t", [EXP, DM, HF], F32, kind="ExternalInput")
    gate_t = nc.dram_tensor("gate_t", [EXP, DM, HF], F32, kind="ExternalInput")
    down_t = nc.dram_tensor("down_t", [EXP, HF, DM], F32, kind="ExternalInput")
    wup_t = nc.dram_tensor("wup_t", [DM, HF], F32, kind="ExternalInput")
    wgate_t = nc.dram_tensor("wgate_t", [DM, HF], F32, kind="ExternalInput")
    wdown_t = nc.dram_tensor("wdown_t", [HF, DM], F32, kind="ExternalInput")
    out = nc.dram_tensor("out", [B, LC, DM], F32, kind="ExternalOutput")

    from contextlib import ExitStack
    with ExitStack() as ctx:
        def sb(name, shp, dt=F32):
            return ctx.enter_context(nc.sbuf_tensor(name, shp, dt))

        woutb = sb("woutb", [128, 16, DM], BF16)
        rtb = sb("rtb", [128, 8, EXP], F32)
        ident_f = sb("ident_f", [128, 128], F32)
        big_bf = sb("big_bf", [128, 16, LC], BF16)
        y_acc = sb("y_acc", [128, 8, DM], F32)
        xn2T_bf = sb("xn2T_bf", [128, 8, LC], BF16)
        y_et = sb("y_et", [128, LC], F32)
        sz_et = sb("sz_et", [128, LC], F32)
        x_tt = sb("x_tt", [128, DM], F32)
        xn2f = sb("xn2f", [128, DM], F32)
        stage = [sb(f"stage{i}", [128, 128], F32) for i in range(2)]
        lgts = sb("lgts", [128, 8, EXP], F32)
        mxc = sb("mxc", [128, 8], F32)
        mask = sb("mask", [128, 8, EXP], F32)
        mraw = sb("mraw", [128, 1], F32)
        mfound = sb("mfound", [128, 8], F32)
        mtmp = sb("mtmp", [128, 1], F32)
        upb = [sb(f"upb{i}", [128, 8, 512], BF16) for i in range(2)]
        gtb = [sb(f"gtb{i}", [128, 8, 512], BF16) for i in range(2)]
        dnb = sb("dnb", [128, 16, DM], BF16)
        sgl = sb("sgl", [128, 512], F32)
        tmp5 = sb("tmp5", [128, 512], F32)
        s1 = sb("s1", [128, 8], F32)
        s2 = sb("s2", [128, 8], F32)
        sqs = sb("sqs", [128, DM], F32)
        mcol = sb("mcol", [128, 8], F32)
        vcol = sb("vcol", [128, 8], F32)
        rcol = sb("rcol", [128, 8], F32)
        tcol = sb("tcol", [128, 8], F32)
        pm = [ctx.enter_context(nc.psum_tensor(f"pm{i}", [128, 512], F32))
              for i in range(4)]
        pTf_full = ctx.enter_context(nc.psum_tensor("pTf", [128, 512], F32))
        pTf = pTf_full[:, 0:128]
        pr_full = ctx.enter_context(nc.psum_tensor("pr", [128, 512], F32))
        pr = pr_full[:, 0:64]

        sm = _sems(nc, ctx, ["LD", "LG", "S", "V", "T", "G", "SO"])
        LD, LG, S, V, T, G, SO = (sm[k] for k in
                                  ["LD", "LG", "S", "V", "T", "G", "SO"])
        block = ctx.enter_context(nc.Block())
        p = Prog()
        pmg = [[], [], [], []]
        pTg = []
        prg = []

        p.emit("gp", lambda g: g.dma_start(
            out=woutb[:], in_=w_out.ap().rearrange("(kt p) e -> p kt e", p=128)),
            incs=[IC(LG, 16)])
        p.emit("sy", lambda s: s.dma_start(
            out=rtb[:], in_=router_t.ap().rearrange("(kc p) e -> p kc e", p=128)),
            incs=[IC(LD, 16)])
        p.emit("gp", lambda g: g.memset(ident_f[:], 0.0), incs=[IC(G)])
        p.emit("gp", lambda g: g.affine_select(
            out=ident_f[:], in_=ident_f[:], compare_op=ALU.not_equal, fill=1.0,
            base=0, pattern=[[-1, 128]], channel_multiplier=1),
            waits=[WT(G)], incs=[IC(G)])

        def emit_ln(src_ap, out_ap, j, extra_waits=()):
            p.emit("v", lambda v, a=src_ap, jj=j: v.reduce_sum(
                s1[:, jj:jj + 1], a, axis=AX),
                waits=list(extra_waits), incs=[IC(V)])
            p.emit("sc", lambda s, a=src_ap, jj=j: s.activation(
                sqs[:], a, AF.Square, accum_out=s2[:, jj:jj + 1]),
                waits=list(extra_waits), incs=[IC(S)])
            p.emit("v", lambda v, jj=j: v.tensor_scalar_mul(
                mcol[:, jj:jj + 1], s1[:, jj:jj + 1], 1.0 / DM),
                waits=[WT(V)], incs=[IC(V)])
            p.emit("v", lambda v, jj=j: v.tensor_scalar_mul(
                vcol[:, jj:jj + 1], s2[:, jj:jj + 1], 1.0 / DM),
                waits=[WT(S)], incs=[IC(V)])
            p.emit("v", lambda v, jj=j: v.tensor_scalar(
                tcol[:, jj:jj + 1], mcol[:, jj:jj + 1], mcol[:, jj:jj + 1],
                None, ALU.mult),
                waits=[WT(V)], incs=[IC(V)])
            p.emit("v", lambda v, jj=j: v.tensor_sub(
                vcol[:, jj:jj + 1], vcol[:, jj:jj + 1], tcol[:, jj:jj + 1]),
                waits=[WT(V)], incs=[IC(V)])
            p.emit("v", lambda v, jj=j: v.tensor_scalar_add(
                vcol[:, jj:jj + 1], vcol[:, jj:jj + 1], EPS),
                waits=[WT(V)], incs=[IC(V)])
            p.emit("sc", lambda s, jj=j: s.activation(
                rcol[:, jj:jj + 1], vcol[:, jj:jj + 1], AF.Ln),
                waits=[WT(V)], incs=[IC(S)])
            p.emit("sc", lambda s, jj=j: s.activation(
                rcol[:, jj:jj + 1], rcol[:, jj:jj + 1], AF.Exp, scale=-0.5),
                waits=[WT(S)], incs=[IC(S)])
            p.emit("v", lambda v, a=src_ap, o=out_ap, jj=j: v.tensor_scalar(
                o, a, mcol[:, jj:jj + 1], rcol[:, jj:jj + 1],
                ALU.subtract, ALU.mult),
                waits=[WT(S)], incs=[IC(V)])

        for b in range(B):
            for et in range(16):
                p.emit("sy", lambda s, b=b, et=et: s.dma_start(
                    out=y_et[:], in_=y_in.ap()[b, bass.ts(et, 128), :]),
                    waits=[WT(V)], incs=[IC(LD, 16)])
                p.emit("sy", lambda s, b=b, et=et: s.dma_start(
                    out=sz_et[:], in_=sz.ap()[b, bass.ts(et, 128), :]),
                    incs=[IC(LD, 16)])
                p.emit("v", lambda v, et=et: v.tensor_mul(
                    big_bf[:, et, :], y_et[:], sz_et[:]),
                    waits=[WT(LD), WT(T)], incs=[IC(V)])
            yg_v = V.n

            for tt in range(8):
                p.emit("sy", lambda s, b=b, tt=tt: s.dma_start(
                    out=x_tt[:], in_=x.ap()[b].rearrange(
                        "(t p) d -> p t d", p=128)[:, tt, :]),
                    waits=[WT(V)], incs=[IC(LD, 16)])
                for f in range(2):
                    pi = (tt * 2 + f) % 2
                    def womm(te, tt=tt, f=f, pi=pi):
                        ins = None
                        for kt in range(16):
                            ins = te.matmul(
                                pm[pi][:], big_bf[:, kt, bass.ts(tt, 128)],
                                woutb[:, kt, bass.ts(f, 512)],
                                start=(kt == 0), stop=(kt == 15))
                        return ins
                    p.emit("te", womm,
                           waits=[WT(V, yg_v), WT(LG, 16)] + pmg[pi],
                           incs=[IC(T)])
                    p.emit("v", lambda v, tt=tt, f=f, pi=pi: v.tensor_add(
                        y_acc[:, tt, bass.ts(f, 512)], pm[pi][:],
                        x_tt[:, bass.ts(f, 512)]),
                        waits=[WT(T), WT(LD), WT(SO)], incs=[IC(V)])
                    pmg[pi] = [WT(V)]

            for tt in range(8):
                emit_ln(y_acc[:, tt, :], xn2f[:], tt,
                        extra_waits=[WT(V), WT(T)])
                xn2_v = V.n
                for kc in range(8):
                    si = kc % 2
                    p.emit("te", lambda te, kc=kc: te.transpose(
                        pTf, xn2f[:, bass.ts(kc, 128)], ident_f[:]),
                        waits=[WT(V, xn2_v), WT(G, 2)] + pTg, incs=[IC(T)])
                    p.emit("sc", lambda s, si=si: s.copy(stage[si][:], pTf),
                           waits=[WT(T)], incs=[IC(S)])
                    pTg = [WT(S)]
                    p.emit("te", lambda te, tt=tt, kc=kc, si=si: te.matmul(
                        pr_full[:, tt * 8:(tt + 1) * 8], stage[si][:], rtb[:, kc, :],
                        start=(kc == 0), stop=(kc == 7)),
                        waits=[WT(S), WT(LD, 16)] + (prg if kc == 0 else []),
                        incs=[IC(T)])
                    p.emit("v", lambda v, tt=tt, kc=kc, si=si: v.tensor_copy(
                        xn2T_bf[:, kc, bass.ts(tt, 128)], stage[si][:]),
                        waits=[WT(S)], incs=[IC(V)])
            xn2T_v = V.n

            p.emit("gp", lambda g: g.memset(mfound[:], 0.0), incs=[IC(G)])
            mf_g = G.n
            for tt in range(8):
                p.emit("v", lambda v, tt=tt: v.tensor_copy(
                    lgts[:, tt, :], pr_full[:, tt * 8:(tt + 1) * 8]),
                    waits=[WT(T)], incs=[IC(V)])
                p.emit("v", lambda v, tt=tt: v.reduce_max(
                    mxc[:, tt:tt + 1], lgts[:, tt, :], axis=AX),
                    waits=[WT(V)], incs=[IC(V)])
                for e in range(EXP):
                    p.emit("v", lambda v, tt=tt, e=e: v.tensor_scalar(
                        mraw[:], lgts[:, tt, e:e + 1], mxc[:, tt:tt + 1],
                        None, ALU.is_equal),
                        waits=[WT(V), WT(G, mf_g)], incs=[IC(V)])
                    p.emit("v", lambda v, tt=tt, e=e: v.tensor_mul(
                        mtmp[:], mraw[:], mfound[:, tt:tt + 1]),
                        waits=[WT(V)], incs=[IC(V)])
                    p.emit("v", lambda v, tt=tt, e=e: v.tensor_sub(
                        mask[:, tt, e:e + 1], mraw[:], mtmp[:]),
                        waits=[WT(V)], incs=[IC(V)])
                    p.emit("v", lambda v, tt=tt, e=e: v.tensor_add(
                        mfound[:, tt:tt + 1], mfound[:, tt:tt + 1],
                        mask[:, tt, e:e + 1]),
                        waits=[WT(V)], incs=[IC(V)])
            prg = [WT(V)]

            for e in range(EXP + 1):
                shared = (e == EXP)
                for hs in range(4):
                    bi = hs % 2
                    if shared:
                        srcu = wup_t.ap()[:, bass.ts(hs, 512)]
                        srcg = wgate_t.ap()[:, bass.ts(hs, 512)]
                    else:
                        srcu = up_t.ap()[e][:, bass.ts(hs, 512)]
                        srcg = gate_t.ap()[e][:, bass.ts(hs, 512)]
                    p.emit("gp", lambda g, srcu=srcu, bi=bi: g.dma_start(
                        out=upb[bi][:],
                        in_=srcu.rearrange("(kc p) h -> p kc h", p=128)),
                        waits=[WT(T)], incs=[IC(LG, 16)])
                    p.emit("gp", lambda g, srcg=srcg, bi=bi: g.dma_start(
                        out=gtb[bi][:],
                        in_=srcg.rearrange("(kc p) h -> p kc h", p=128)),
                        incs=[IC(LG, 16)])
                    lg_mark = LG.n
                    for hm in range(4):
                        ht = hs * 4 + hm
                        for f in range(2):
                            def umm(te, bi=bi, hm=hm, f=f):
                                ins = None
                                for kc in range(8):
                                    ins = te.matmul(
                                        pm[0][:],
                                        upb[bi][:, kc, bass.ts(hm, 128)],
                                        xn2T_bf[:, kc, bass.ts(f, 512)],
                                        start=(kc == 0), stop=(kc == 7))
                                return ins
                            def gmm(te, bi=bi, hm=hm, f=f):
                                ins = None
                                for kc in range(8):
                                    ins = te.matmul(
                                        pm[1][:],
                                        gtb[bi][:, kc, bass.ts(hm, 128)],
                                        xn2T_bf[:, kc, bass.ts(f, 512)],
                                        start=(kc == 0), stop=(kc == 7))
                                return ins
                            p.emit("te", umm,
                                   waits=[WT(V, xn2T_v), WT(LG, lg_mark)]
                                   + pmg[0], incs=[IC(T)])
                            p.emit("te", gmm, waits=pmg[1], incs=[IC(T)])
                            p.emit("sc", lambda s: s.activation(
                                sgl[:], pm[1][:], AF.Silu),
                                waits=[WT(T)], incs=[IC(S)])
                            pmg[1] = [WT(S)]
                            p.emit("v", lambda v, ht=ht, f=f: v.tensor_mul(
                                big_bf[:, ht, bass.ts(f, 512)], sgl[:],
                                pm[0][:]),
                                waits=[WT(S), WT(T)], incs=[IC(V)])
                            pmg[0] = [WT(V)]
                he_v = V.n
                srcd = wdown_t.ap() if shared else down_t.ap()[e]
                p.emit("gp", lambda g, srcd=srcd: g.dma_start(
                    out=dnb[:], in_=srcd.rearrange("(kt p) c -> p kt c", p=128)),
                    waits=[WT(T)], incs=[IC(LG, 16)])
                dn_mark = LG.n
                for tt in range(8):
                    for f in range(2):
                        pi = 2 + (tt * 2 + f) % 2
                        def dmm(te, tt=tt, f=f, pi=pi):
                            ins = None
                            for kt in range(16):
                                ins = te.matmul(
                                    pm[pi][:], big_bf[:, kt, bass.ts(tt, 128)],
                                    dnb[:, kt, bass.ts(f, 512)],
                                    start=(kt == 0), stop=(kt == 15))
                            return ins
                        p.emit("te", dmm,
                               waits=[WT(V, he_v), WT(LG, dn_mark)] + pmg[pi],
                               incs=[IC(T)])
                        if shared:
                            p.emit("v", lambda v, tt=tt, f=f, pi=pi:
                                   v.tensor_add(
                                       y_acc[:, tt, bass.ts(f, 512)],
                                       y_acc[:, tt, bass.ts(f, 512)],
                                       pm[pi][:]),
                                   waits=[WT(T), WT(V)], incs=[IC(V)])
                            pmg[pi] = [WT(V)]
                        else:
                            p.emit("v", lambda v, tt=tt, e=e, pi=pi:
                                   v.tensor_scalar(
                                       tmp5[:], pm[pi][:],
                                       mask[:, tt, e:e + 1], None, ALU.mult),
                                   waits=[WT(T), WT(V)], incs=[IC(V)])
                            pmg[pi] = [WT(V)]
                            p.emit("v", lambda v, tt=tt, f=f: v.tensor_add(
                                y_acc[:, tt, bass.ts(f, 512)],
                                y_acc[:, tt, bass.ts(f, 512)], tmp5[:]),
                                waits=[WT(V)], incs=[IC(V)])

            p.emit("sy", lambda s, b=b: s.dma_start(
                out=out.ap()[b].rearrange("(t p) d -> p t d", p=128),
                in_=y_acc[:]),
                waits=[WT(V)], incs=[IC(SO, 16)])

        p.emit("sy", lambda s: None, waits=[WT(SO)])
        p.run(block)
    return nc


def kernel(L0_attn_x, L0_attn_W_in, L0_attn_conv_w, L0_attn_conv_b,
           L0_attn_A_log, L0_attn_D, L0_attn_W_x, L0_attn_W_dt, L0_attn_b_dt,
           L0_attn_W_out, L0_ffn_up, L0_ffn_gate, L0_ffn_down, L0_ffn_router,
           L0_ffn_w_up_s, L0_ffn_w_gate_s, L0_ffn_w_down_s):
    f32 = np.float32
    xx = np.asarray(L0_attn_x, f32)
    w_in = np.ascontiguousarray(np.asarray(L0_attn_W_in, f32))
    convw = np.ascontiguousarray(np.asarray(L0_attn_conv_w, f32).reshape(16, 128, 4).transpose(1, 0, 2).reshape(128, 64))
    convb = np.ascontiguousarray(np.asarray(L0_attn_conv_b, f32).reshape(16, 128).T)
    alog = np.ascontiguousarray(np.asarray(L0_attn_A_log, f32).reshape(1, NS))
    dvec = np.asarray(L0_attn_D, f32)
    w_x = np.ascontiguousarray(np.asarray(L0_attn_W_x, f32).reshape(16, 128, 96).transpose(1, 0, 2).reshape(128, 16 * 96))
    w_dt = np.ascontiguousarray(np.asarray(L0_attn_W_dt, f32))
    b_dt = np.ascontiguousarray(np.asarray(L0_attn_b_dt, f32).reshape(16, 128).T)
    w_out = np.ascontiguousarray(np.asarray(L0_attn_W_out, f32))
    up_t = np.ascontiguousarray(np.asarray(L0_ffn_up, f32).transpose(0, 2, 1))
    gate_t = np.ascontiguousarray(np.asarray(L0_ffn_gate, f32).transpose(0, 2, 1))
    down_t = np.ascontiguousarray(np.asarray(L0_ffn_down, f32).transpose(0, 2, 1))
    router_t = np.ascontiguousarray(np.asarray(L0_ffn_router, f32).T)
    wup_t = np.ascontiguousarray(np.asarray(L0_ffn_w_up_s, f32).T)
    wgate_t = np.ascontiguousarray(np.asarray(L0_ffn_w_gate_s, f32).T)
    wdown_t = np.ascontiguousarray(np.asarray(L0_ffn_w_down_s, f32).T)

    nc1 = build_l1()
    in1 = []
    for c in range(NCORE):
        xc = np.ascontiguousarray(xx[:, c * LC:(c + 1) * LC, :])
        if c == 0:
            xhc = np.zeros((B, 3, DM), f32)
        else:
            xhc = np.ascontiguousarray(xx[:, c * LC - 3:c * LC, :])
        in1.append({"x": xc, "xh": xhc, "w_in": w_in, "convw": convw,
                    "convb": convb, "w_x": w_x, "w_dt": w_dt, "b_dt": b_dt})
    r1 = run_bass_kernel_spmd(nc1, in1, core_ids=list(range(NCORE)))
    a2a = [r1.results[c]["a2a"] for c in range(NCORE)]
    bc = [r1.results[c]["bc"] for c in range(NCORE)]
    szs = [r1.results[c]["sz"] for c in range(NCORE)]

    nc2 = build_l2()
    in2 = []
    for s in range(NCORE):
        u_sc = np.concatenate(
            [a2a[c][:, s * 512:s * 512 + 256, :] for c in range(NCORE)], axis=2)
        dt_sc = np.concatenate(
            [a2a[c][:, s * 512 + 256:(s + 1) * 512, :] for c in range(NCORE)],
            axis=2)
        ball = np.concatenate([bc[c][:, 0:NS, :] for c in range(NCORE)], axis=2)
        call = np.concatenate([bc[c][:, NS:2 * NS, :] for c in range(NCORE)],
                              axis=2)
        d_cs = np.ascontiguousarray(
            dvec[s * DSL:(s + 1) * DSL].reshape(2, 128).T)
        in2.append({"u_s": np.ascontiguousarray(u_sc),
                    "dt_s": np.ascontiguousarray(dt_sc),
                    "b_all": np.ascontiguousarray(ball),
                    "c_all": np.ascontiguousarray(call),
                    "alog": alog, "d_c": d_cs})
    r2 = run_bass_kernel_spmd(nc2, in2, core_ids=list(range(NCORE)))
    y_s = [r2.results[s]["y_s"] for s in range(NCORE)]

    nc3 = build_l3()
    in3 = []
    for c in range(NCORE):
        y_c = np.concatenate(
            [y_s[s][:, :, c * LC:(c + 1) * LC] for s in range(NCORE)], axis=1)
        xc = np.ascontiguousarray(xx[:, c * LC:(c + 1) * LC, :])
        in3.append({"x": xc, "y_in": np.ascontiguousarray(y_c),
                    "sz": szs[c], "w_out": w_out, "router_t": router_t,
                    "up_t": up_t, "gate_t": gate_t, "down_t": down_t,
                    "wup_t": wup_t, "wgate_t": wgate_t, "wdown_t": wdown_t})
    r3 = run_bass_kernel_spmd(nc3, in3, core_ids=list(range(NCORE)))

    outp = np.empty((B, L, DM), f32)
    for c in range(NCORE):
        outp[:, c * LC:(c + 1) * LC, :] = r3.results[c]["out"]
    return outp
